# version 02
# baseline (speedup 1.0000x reference)
import numpy as np
import ml_dtypes
import concourse.bass as bass
import concourse.mybir as mybir
from concourse.bass_utils import run_bass_kernel_spmd

F32 = mybir.dt.float32
BF16 = mybir.dt.bfloat16
ALU = mybir.AluOpType
AF = mybir.ActivationFunctionType
AX = mybir.AxisListType

D = 1024
S = 2048
NSEQ = 2
T = S * NSEQ
NC8 = 8
EPS = 1e-5



def _f32(a):
    return np.ascontiguousarray(a, dtype=np.float32)


def _rope_tables_L0():
    j = np.arange(0, 128, 2, dtype=np.float32)
    inv = (1.0 / (10000.0 ** (j / np.float32(128.0)))).astype(np.float32)
    t = np.arange(S, dtype=np.float32)[:, None]
    fr = (t * inv[None, :]).astype(np.float32)
    emb = np.concatenate([fr, fr], axis=1)
    cos = np.cos(emb).astype(np.float32)
    sin = np.sin(emb).astype(np.float32)
    ss = sin.copy()
    ss[:, 0:64] *= -1.0
    return _f32(cos.T), _f32(ss.T)


def _perm_L1():
    perm = np.zeros(D, dtype=np.int64)
    for j in range(D):
        h, r = j // 64, j % 64
        perm[j] = 64 * h + (2 * r if r < 32 else 2 * (r - 32) + 1)
    return perm


def _rope_tables_L1():
    ids = np.arange(D // 2, dtype=np.float32)
    theta = ((np.float32(10000.0) ** (np.float32(-2.0) * ids))
             / np.float32(D // 2)).astype(np.float32)
    t = np.arange(S, dtype=np.float32)[:, None]
    r = (t * theta[None, :]).astype(np.float32)
    cos = np.cos(r).astype(np.float32)
    sin = np.sin(r).astype(np.float32)
    C = np.zeros((D, S), dtype=np.float32)
    Ss = np.zeros((D, S), dtype=np.float32)
    for j in range(D):
        h, r_ = j // 64, j % 64
        ip = 32 * h + (r_ if r_ < 32 else r_ - 32)
        C[j] = cos[:, ip]
        Ss[j] = (-sin[:, ip]) if r_ < 32 else sin[:, ip]
    return _f32(C), _f32(Ss)


def prep_shared(inp):
    sh = {}
    for nm in ["Wq", "Wk", "Wv", "Wg", "Wo"]:
        w = np.asarray(inp[f"L0_attn_{nm}"])
        sh[f"l0_{nm.lower()}_t"] = _f32(w.T).reshape(NC8, 128, D)
    lw = np.asarray(inp["L0_attn_lepe_w"]).reshape(D, 5)
    sh["l0_lepe_w"] = _f32(lw.reshape(NC8, 128, 5).transpose(1, 0, 2))
    sh["l0_lepe_b"] = _f32(np.asarray(inp["L0_attn_lepe_b"]).reshape(NC8, 128).T)
    c0, s0 = _rope_tables_L0()
    sh["l0_ropec"] = c0
    sh["l0_ropes"] = s0
    sh["gate_w"] = _f32(np.asarray(inp["L0_ffn_w_gate"])).reshape(NC8, 128, 8)
    sh["gate_b"] = _f32(np.asarray(inp["L0_ffn_b_gate"])).reshape(1, 8)
    sh["w1_t"] = _f32(np.asarray(inp["L0_ffn_W1"])).reshape(8, NC8, 128, 4096)
    sh["b1_r"] = _f32(np.asarray(inp["L0_ffn_b1"])).reshape(8, 1, 4096)
    sh["w2_t"] = _f32(np.asarray(inp["L0_ffn_W2"])).reshape(8, 32, 128, D)
    sh["b2_r"] = _f32(np.asarray(inp["L0_ffn_b2"]))
    P = _perm_L1()
    g = lambda k: np.asarray(inp[f"L1_attn_{k}"])
    sh["wdq_t"] = _f32(g("Wd_q_w")).reshape(NC8, 128, 512)
    sh["bdq_c"] = _f32(g("Wd_q_b").reshape(4, 128).T)
    sh["wuq_t"] = _f32(g("Wu_q_w")[:, P]).reshape(4, 128, D)
    sh["buq_c"] = _f32(g("Wu_q_b")[P].reshape(NC8, 128).T)
    sh["wqr_t"] = _f32(g("Wq_r_w")[:, P]).reshape(4, 128, D)
    sh["bqr_c"] = _f32(g("Wq_r_b")[P].reshape(NC8, 128).T)
    sh["wkr_t"] = _f32(g("Wk_r_w")[:, P]).reshape(NC8, 128, D)
    sh["bkr_c"] = _f32(g("Wk_r_b")[P].reshape(NC8, 128).T)
    sh["wdkv_t"] = _f32(g("Wd_kv_w")).reshape(NC8, 128, 512)
    sh["bdkv_c"] = _f32(g("Wd_kv_b").reshape(4, 128).T)
    sh["wuk_t"] = _f32(g("Wu_k_w")[:, P]).reshape(4, 128, D)
    sh["buk_c"] = _f32(g("Wu_k_b")[P].reshape(NC8, 128).T)
    sh["wuv_t"] = _f32(g("Wu_v_w")).reshape(4, 128, D)
    sh["buv_r"] = _f32(g("Wu_v_b")).reshape(1, D)
    sh["wo1_t"] = _f32(g("Wo_w")).reshape(NC8, 128, D)
    sh["wob_c"] = _f32(g("Wo_b").reshape(NC8, 128).T)
    c1, s1 = _rope_tables_L1()
    sh["l1_ropec"] = c1.reshape(NC8, 128, S)
    sh["l1_ropes"] = s1.reshape(NC8, 128, S)
    rm = np.asarray(inp["L1_ffn_rail_maker"])
    rup = np.asarray(inp["L1_ffn_rail_up_proj"])
    rmT = np.zeros((4, NC8, 128, 128), dtype=np.float32)
    rupT = np.zeros((4, NC8, 128, 512), dtype=np.float32)
    for h in range(4):
        for c in range(NC8):
            for r8 in range(8):
                r = 8 * c + r8
                rmT[h, c, 16 * r8:16 * r8 + 16, 16 * r8:16 * r8 + 16] = rm[h, r].T
                rupT[h, c, 16 * r8:16 * r8 + 16, 64 * r8:64 * r8 + 64] = rup[h, r].T
    sh["rmT_bd"] = rmT
    sh["rupT_bd"] = rupT
    sh["wc_t"] = _f32(np.asarray(inp["L1_ffn_w_cproj"]).T).reshape(128, 128, D)
    sh["iden_bf"] = np.eye(128, dtype=ml_dtypes.bfloat16)
    sh["ones_col_bf"] = np.ones((128, 1), dtype=ml_dtypes.bfloat16)
    sh["ones_row_bf"] = np.ones((1, 128), dtype=ml_dtypes.bfloat16)
    sh["ones_row_f"] = np.ones((1, 128), dtype=np.float32)
    sel = np.zeros((8, 8, 128), dtype=ml_dtypes.bfloat16)
    for i in range(8):
        sel[i, i, :] = 1.0
    sh["sel_bf"] = sel
    return sh


def prep_core_x(x, core):
    xc = np.concatenate([x[2 * core], x[2 * core + 1]], axis=0)
    return _f32(xc.T).reshape(NC8, 128, T)


def finish_out(out_fm_list):
    outs = []
    for o in out_fm_list:
        o2 = np.asarray(o).reshape(D, T).T
        outs.append(o2[:S])
        outs.append(o2[S:])
    return np.stack(outs, axis=0).astype(np.float32)



def emit_ln_stripe(block, B, c, col0, dst_hbm, dstbf_hbm):
    zpre, zbf, zsq = B["zpre"], B["zbf"], B["zsq"]
    mean, ms, var, r_, rtmp = (B["row_mean"], B["row_ms"], B["row_var"],
                               B["row_r"], B["row_tmp"])
    ones_c, ones_rf = B["ones_col_bf"], B["ones_row_f"]
    epsc = B["epsc"]
    pS, pQ, pR, pM = B["pS"], B["pQ"], B["pR"], B["pM"]
    ld, mm, ev, vx = B["s_ld"], B["s_mm"], B["s_ev"], B["s_vx"]
    stS, stG = B["s_stS"], B["s_stG"]

    rd_sem, rd_val = B.get("zpre_ready", (vx, None))
    if rd_val is None:
        rd_val = c["vx"]

    @block.scalar
    def _(sc):
        sc.wait_ge(rd_sem, rd_val)
        sc.wait_ge(stS, c["stS"])
        sc.wait_ge(stG, c["stG"])
        sc.copy(zbf[:, :, :], zpre[:, :, :]).then_inc(ev, 1)
        sc.activation(zsq[:, :, :], zpre[:, :, :], AF.Square).then_inc(ev, 1)
    c["ev"] += 2

    @block.tensor
    def _(t):
        t.wait_ge(ld, B.get("ld_static", 0))
        t.wait_ge(ev, c["ev"])
        for cc in range(NC8):
            t.matmul(pS[0:1, :], ones_c[:, :], zbf[:, cc, :],
                     start=(cc == 0), stop=(cc == NC8 - 1))
            mi = t.matmul(pQ[0:1, :], ones_c[:, :], zsq[:, cc, :],
                          start=(cc == 0), stop=(cc == NC8 - 1))
        mi.then_inc(mm, 1)
    c["mm"] += 1

    @block.scalar
    def _(sc):
        sc.wait_ge(mm, c["mm"])
        sc.activation(mean[:, :], pS[0:1, :], AF.Copy, scale=1.0 / D).then_inc(ev, 1)
        sc.activation(ms[:, :], pQ[0:1, :], AF.Copy, scale=1.0 / D).then_inc(ev, 1)
    c["ev"] += 2

    @block.vector
    def _(v):
        v.wait_ge(ev, c["ev"])
        v.tensor_tensor(rtmp[:, :], mean[:, :], mean[:, :], ALU.mult).then_inc(vx, 1)
        v.wait_ge(vx, c["vx"] + 1)
        v.tensor_tensor(var[:, :], ms[:, :], rtmp[:, :], ALU.subtract).then_inc(vx, 1)
    c["vx"] += 2

    @block.scalar
    def _(sc):
        sc.wait_ge(vx, c["vx"])
        sc.activation(rtmp[:, :], var[:, :], AF.Ln, bias=epsc[0:1, 0:1]).then_inc(ev, 1)
        sc.wait_ge(ev, c["ev"] + 1)
        sc.activation(r_[:, :], rtmp[:, :], AF.Exp, scale=-0.5).then_inc(ev, 1)
    c["ev"] += 2

    @block.tensor
    def _(t):
        t.wait_ge(ev, c["ev"])
        t.matmul(pR[:, :], ones_rf[:, :], r_[:, :], start=True, stop=True)
        t.matmul(pM[:, :], ones_rf[:, :], mean[:, :],
                 start=True, stop=True).then_inc(mm, 1)
    c["mm"] += 1

    @block.vector
    def _(v):
        v.wait_ge(mm, c["mm"])
        for cc in range(NC8):
            ci = v.tensor_tensor(zpre[:, cc, :], zpre[:, cc, :], pM[:, :], ALU.subtract)
        ci.then_inc(vx, 1)
        v.wait_ge(vx, c["vx"] + 1)
        for cc in range(NC8):
            ci = v.tensor_tensor(zpre[:, cc, :], zpre[:, cc, :], pR[:, :], ALU.mult)
        ci.then_inc(vx, 1)
    c["vx"] += 2

    @block.sync
    def _(sy):
        sy.wait_ge(vx, c["vx"])
        sy.dma_start(out=dst_hbm[:, :, col0:col0 + 512]
                     .rearrange("c p t -> p c t"), in_=zpre[:, :, :]).then_inc(stS, 16)
    c["stS"] += 16

    if dstbf_hbm is not None:
        @block.gpsimd
        def _(g):
            g.wait_ge(vx, c["vx"])
            g.dma_start(out=dstbf_hbm[:, :, col0:col0 + 512]
                        .rearrange("c p t -> p c t"), in_=zpre[:, :, :]).then_inc(stG, 16)
        c["stG"] += 16





QSCALE = float(np.float32(128.0 ** -0.5))


def build_launch_a():
    nc = bass.Bass("TRN2", target_bir_lowering=False)
    x_fm = nc.dram_tensor("x_fm", [NC8, 128, T], F32, kind="ExternalInput").ap()
    wq = nc.dram_tensor("l0_wq_t", [NC8, 128, D], F32, kind="ExternalInput").ap()
    wk = nc.dram_tensor("l0_wk_t", [NC8, 128, D], F32, kind="ExternalInput").ap()
    wv = nc.dram_tensor("l0_wv_t", [NC8, 128, D], F32, kind="ExternalInput").ap()
    wg = nc.dram_tensor("l0_wg_t", [NC8, 128, D], F32, kind="ExternalInput").ap()
    wo = nc.dram_tensor("l0_wo_t", [NC8, 128, D], F32, kind="ExternalInput").ap()
    lepw_h = nc.dram_tensor("l0_lepe_w", [128, NC8, 5], F32, kind="ExternalInput").ap()
    lepb_h = nc.dram_tensor("l0_lepe_b", [128, NC8], F32, kind="ExternalInput").ap()
    ropec_h = nc.dram_tensor("l0_ropec", [128, S], F32, kind="ExternalInput").ap()
    ropes_h = nc.dram_tensor("l0_ropes", [128, S], F32, kind="ExternalInput").ap()
    iden_h = nc.dram_tensor("iden_bf", [128, 128], BF16, kind="ExternalInput").ap()
    onc_h = nc.dram_tensor("ones_col_bf", [128, 1], BF16, kind="ExternalInput").ap()
    onrf_h = nc.dram_tensor("ones_row_f", [1, 128], F32, kind="ExternalInput").ap()
    z1_h = nc.dram_tensor("z1_fm", [NC8, 128, T], F32, kind="ExternalOutput").ap()
    z1b_h = nc.dram_tensor("z1_bf", [NC8, 128, T], BF16, kind="ExternalOutput").ap()

    import contextlib
    es = contextlib.ExitStack()
    def sb(name, shape, dtype):
        return es.enter_context(nc.sbuf_tensor(name, shape, dtype))
    def ps(name):
        return es.enter_context(nc.psum_tensor(name, [128, 512], F32))
    def sem(name):
        return es.enter_context(nc.semaphore(name))
    xbf = sb("xbf", [128, NC8, S], BF16)
    wsl = sb("wsl", [128, 2, NC8, 128], BF16)
    vfm = sb("vfm", [128, NC8, S + 4], BF16)
    ropeC = sb("ropeC", [128, S], BF16)
    ropeS = sb("ropeS", [128, S], BF16)
    kpre = sb("kpre", [128, 512], BF16)
    swstg = sb("swstg", [128, 512], BF16)
    rstgA = sb("rstgA", [128, 512], F32)
    rstgB = sb("rstgB", [128, 512], F32)
    kwsl = sb("kwsl", [128, 512], BF16)
    ktm = sb("ktm", [128, 2, 128], BF16)
    vtm = sb("vtm", [128, 2, 129], BF16)
    kvbf = sb("kvbf", [128, NC8, 129], BF16)
    vsbf = sb("vsbf", [1, NC8, 128], BF16)
    qw = sb("qw", [128, NC8, 512], BF16)
    gw = sb("gw", [128, NC8, 512], BF16)
    osb = sb("osb", [128, NC8, 512], F32)
    zpre = sb("zpre", [128, NC8, 512], F32)
    lepacc = zpre
    obf = sb("obf", [128, NC8, 512], BF16)
    xres = sb("xres", [128, NC8, 512], F32)
    numst = sb("numst", [128, 512], F32)
    denr = sb("denr", [1, 512], F32)
    denri = sb("denri", [1, 512], F32)
    lepe_wc = sb("lepe_wc", [128, NC8, 5], F32)
    lepe_bc = sb("lepe_bc", [128, NC8], F32)
    iden = sb("iden_sb", [128, 128], BF16)
    ones_col = sb("ones_col_sb", [128, 1], BF16)
    ones_rf = sb("ones_row_sb", [1, 128], F32)
    ones512 = sb("ones512", [1, 512], BF16)
    epsc = sb("epsc", [1, 1], F32)
    zbf = sb("zbf", [128, NC8, 512], BF16)
    zsq = sb("zsq", [128, NC8, 512], BF16)
    row_mean = sb("row_mean", [1, 512], F32)
    row_ms = sb("row_ms", [1, 512], F32)
    row_var = sb("row_var", [1, 512], F32)
    row_r = sb("row_r", [1, 512], F32)
    row_tmp = sb("row_tmp", [1, 512], F32)
    pb0, pb1, pb2, pb3 = ps("pb0"), ps("pb1"), ps("pb2"), ps("pb3")
    pb4, pb5, pb6, pb7 = ps("pb4"), ps("pb5"), ps("pb6"), ps("pb7")
    ld, mm = sem("ld"), sem("mm")
    glt, glx, gl0, gl1 = sem("glt"), sem("glx"), sem("gl0"), sem("gl1")
    ev, vx = sem("ev"), sem("vx")
    stS, stG = sem("stS"), sem("stG")
    with es, nc.Block() as block:
        c = dict(ld=0, glt=0, glx=0, gl0=0, gl1=0, mm=0, ev=0, vx=0, stS=0, stG=0)
        GL = (gl0, gl1)
        B = dict(zpre=zpre, zbf=zbf, zsq=zsq, row_mean=row_mean, row_ms=row_ms,
                 row_var=row_var, row_r=row_r, row_tmp=row_tmp,
                 ones_col_bf=ones_col, ones_row_f=ones_rf, epsc=epsc,
                 pS=pb6, pQ=pb7, pR=pb3, pM=pb4, ld_static=80,
                 s_ld=ld, s_mm=mm, s_ev=ev, s_vx=vx, s_stS=stS, s_stG=stG)
        pbt = pb2.bitcast(BF16)

        @block.sync
        def _(sy):
            sy.dma_start(out=iden[:, :], in_=iden_h).then_inc(ld, 16)
            sy.dma_start(out=ones_col[:, :], in_=onc_h).then_inc(ld, 16)
            sy.dma_start(out=ones_rf[:, :], in_=onrf_h).then_inc(ld, 16)
            sy.dma_start(out=lepe_wc[:, :, :], in_=lepw_h).then_inc(ld, 16)
            sy.dma_start(out=lepe_bc[:, :], in_=lepb_h).then_inc(ld, 16)
        c["ld"] += 80

        @block.gpsimd
        def _(g):
            g.dma_start(out=ropeC[:, :], in_=ropec_h).then_inc(glt, 16)
            g.dma_start(out=ropeS[:, :], in_=ropes_h).then_inc(glt, 16)
        c["glt"] += 32

        @block.vector
        def _(v):
            v.memset(vfm[:, :, 0:2], 0.0)
            v.memset(vfm[:, :, S + 2:S + 4], 0.0)
            v.memset(vtm[:, 0, 128:129], 1.0)
            v.memset(vtm[:, 1, 128:129], 1.0)
            v.memset(ones512[:, :], 1.0)
            mi = v.memset(epsc[:, :], EPS)
            mi.then_inc(vx, 1)
        c["vx"] += 1

        wsl_free_mm = [0, 0]

        for s in range(NSEQ):
            @block.gpsimd
            def _(g, s=s):
                if s == 1:
                    g.wait_ge(mm, c["mm"])
                g.dma_start(out=xbf[:, :, :],
                            in_=x_fm[:, :, s * S:(s + 1) * S]
                            .rearrange("c p t -> p c t")).then_inc(glx, 16)
            c["glx"] += 16

            for oc in range(NC8):
                b = oc % 2

                @block.gpsimd
                def _(g, oc=oc, b=b):
                    g.wait_ge(mm, wsl_free_mm[b])
                    g.dma_start(out=wsl[:, b, :, :],
                                in_=wv[:, :, 128 * oc:128 * oc + 128]
                                .rearrange("c p o -> p c o")).then_inc(GL[b], 16)
                c["gl%d" % b] += 16

                ev_base = c["ev"]

                @block.tensor
                def _(t, oc=oc, b=b, ev_base=ev_base):
                    t.wait_ge(GL[b], c["gl%d" % b])
                    t.wait_ge(glx, c["glx"])
                    t.wait_ge(ev, ev_base)
                    for w in range(4):
                        pband = pb0 if (w % 2 == 0) else pb1
                        if w >= 2:
                            t.wait_ge(ev, ev_base + (w - 2) + 1)
                        for cc in range(NC8):
                            mi = t.matmul(pband[:, :], wsl[:, b, cc, :],
                                          xbf[:, cc, 512 * w:512 * w + 512],
                                          start=(cc == 0), stop=(cc == NC8 - 1))
                            if cc == NC8 - 1:
                                mi.then_inc(mm, 1)
                wsl_free_mm[b] = c["mm"] + 4
                c["mm"] += 4

                @block.scalar
                def _(sc, oc=oc, s=s):
                    for w in range(4):
                        pband = pb0 if (w % 2 == 0) else pb1
                        sc.wait_ge(mm, c["mm"] - 3 + w)
                        sc.copy(vfm[:, oc, 2 + 512 * w:2 + 512 * w + 512],
                                pband[:, :]).then_inc(ev, 1)
                c["ev"] += 4

            kv_started = False
            for h in range(NC8):
                b = h % 2

                @block.gpsimd
                def _(g, h=h, b=b):
                    g.wait_ge(mm, wsl_free_mm[b])
                    g.dma_start(out=wsl[:, b, :, :],
                                in_=wk[:, :, 128 * h:128 * h + 128]
                                .rearrange("c p o -> p c o")).then_inc(GL[b], 16)
                c["gl%d" % b] += 16

                for w in range(4):
                    @block.tensor
                    def _(t, h=h, b=b, w=w):
                        t.wait_ge(GL[b], c["gl%d" % b])
                        t.wait_ge(glx, c["glx"])
                        t.wait_ge(ev, c["ev"])
                        if s == 0 and h == 0 and w == 0:
                            t.wait_ge(vx, 1)
                        for cc in range(NC8):
                            mi = t.matmul(pb0[:, :], wsl[:, b, cc, :],
                                          xbf[:, cc, 512 * w:512 * w + 512],
                                          start=(cc == 0), stop=(cc == NC8 - 1))
                            if cc == NC8 - 1:
                                mi.then_inc(mm, 1)
                    c["mm"] += 1
                    wsl_free_mm[b] = c["mm"]

                    @block.scalar
                    def _(sc, w=w):
                        sc.wait_ge(mm, c["mm"])
                        sc.wait_ge(vx, c["vx"])
                        sc.copy(kpre[:, :], pb0[:, :]).then_inc(ev, 1)
                        sc.wait_ge(ev, c["ev"] + 1)
                        sc.copy(swstg[0:64, :], kpre[64:128, :]).then_inc(ev, 1)
                        sc.copy(swstg[64:128, :], kpre[0:64, :]).then_inc(ev, 1)
                    c["ev"] += 3

                    @block.vector
                    def _(v, w=w):
                        v.wait_ge(ev, c["ev"])
                        v.wait_ge(glt, 32)
                        v.tensor_tensor(rstgA[:, :], kpre[:, :],
                                        ropeC[:, 512 * w:512 * w + 512],
                                        ALU.mult).then_inc(vx, 1)
                        v.tensor_tensor(rstgB[:, :], swstg[:, :],
                                        ropeS[:, 512 * w:512 * w + 512],
                                        ALU.mult).then_inc(vx, 1)
                        v.wait_ge(vx, c["vx"] + 2)
                        v.tensor_tensor(kwsl[:, :], rstgA[:, :], rstgB[:, :],
                                        ALU.add).then_inc(vx, 1)
                    c["vx"] += 3

                    for j in range(4):
                        jb = j % 2

                        @block.tensor
                        def _(t, h=h, w=w, j=j, jb=jb):
                            t.wait_ge(ld, 80)
                            t.wait_ge(vx, c["vx"])
                            t.wait_ge(ev, c["ev"])
                            t.transpose(pbt[:, 0:128], kwsl[:, 128 * j:128 * j + 128],
                                        iden[:, :])
                            t.transpose(pbt[:, 128:256],
                                        vfm[:, h, 2 + 512 * w + 128 * j:
                                            2 + 512 * w + 128 * j + 128],
                                        iden[:, :]).then_inc(mm, 1)
                        c["mm"] += 1

                        @block.scalar
                        def _(sc, jb=jb):
                            sc.wait_ge(mm, c["mm"])
                            sc.copy(ktm[:, jb, :], pbt[:, 0:128]).then_inc(ev, 1)
                            sc.copy(vtm[:, jb, 0:128], pbt[:, 128:256]).then_inc(ev, 1)
                        c["ev"] += 2

                        @block.tensor
                        def _(t, h=h, w=w, j=j, jb=jb, kv_started=kv_started):
                            t.wait_ge(ev, c["ev"])
                            first = (w == 0 and j == 0)
                            last = (w == 3 and j == 3)
                            kvb = (pb3, pb4, pb5)[h // 3]
                            col = 129 * (h % 3)
                            t.matmul(kvb[:, col:col + 129], ktm[:, jb, :],
                                     vtm[:, jb, 0:129], start=first, stop=last)
                            vsb = (pb6, pb7)[h // 4]
                            vc = 128 * (h % 4)
                            t.matmul(vsb[0:1, vc:vc + 128], ones_col[:, :],
                                     vtm[:, jb, 0:128], start=first,
                                     stop=last).then_inc(mm, 1)
                        c["mm"] += 1

            @block.scalar
            def _(sc):
                sc.wait_ge(mm, c["mm"])
                for h in range(NC8):
                    kvb = (pb3, pb4, pb5)[h // 3]
                    col = 129 * (h % 3)
                    ci = sc.copy(kvbf[:, h, :], kvb[:, col:col + 129])
                ci.then_inc(ev, 1)
            c["ev"] += 1

            @block.vector
            def _(v):
                v.wait_ge(mm, c["mm"])
                for h in range(NC8):
                    vsb = (pb6, pb7)[h // 4]
                    vc = 128 * (h % 4)
                    ci = v.tensor_copy(vsbf[0:1, h, :], vsb[0:1, vc:vc + 128])
                ci.then_inc(vx, 1)
            c["vx"] += 1

            for w in range(4):
                col0 = s * S + 512 * w

                @block.sync
                def _(sy, w=w, col0=col0):
                    sy.wait_ge(vx, c["vx"])
                    sy.dma_start(out=xres[:, :, :],
                                 in_=x_fm[:, :, col0:col0 + 512]
                                 .rearrange("c p t -> p c t")).then_inc(ld, 16)
                c["ld"] += 16

                for kind in ("g", "q"):
                    wsrc = wg if kind == "g" else wq
                    pband = pb0 if kind == "g" else pb1
                    dstsb = gw if kind == "g" else qw
                    for oc in range(NC8):
                        b = oc % 2

                        @block.gpsimd
                        def _(g_, oc=oc, b=b, wsrc=wsrc):
                            g_.wait_ge(mm, wsl_free_mm[b])
                            g_.dma_start(out=wsl[:, b, :, :],
                                         in_=wsrc[:, :, 128 * oc:128 * oc + 128]
                                         .rearrange("c p o -> p c o")).then_inc(GL[b], 16)
                        c["gl%d" % b] += 16

                        @block.tensor
                        def _(t, oc=oc, b=b, w=w, pband=pband):
                            t.wait_ge(GL[b], c["gl%d" % b])
                            t.wait_ge(glx, c["glx"])
                            t.wait_ge(ev, c["ev"])
                            for cc in range(NC8):
                                mi = t.matmul(pband[:, :], wsl[:, b, cc, :],
                                              xbf[:, cc, 512 * w:512 * w + 512],
                                              start=(cc == 0), stop=(cc == NC8 - 1))
                                if cc == NC8 - 1:
                                    mi.then_inc(mm, 1)
                        wsl_free_mm[b] = c["mm"] + 1
                        c["mm"] += 1

                        if kind == "g":
                            @block.scalar
                            def _(sc, oc=oc, pband=pband, dstsb=dstsb):
                                sc.wait_ge(mm, c["mm"])
                                sc.copy(dstsb[:, oc, :], pband[:, :]).then_inc(ev, 1)
                            c["ev"] += 1
                        else:
                            @block.scalar
                            def _(sc, oc=oc, pband=pband):
                                sc.wait_ge(mm, c["mm"])
                                sc.wait_ge(vx, c["vx"])
                                sc.activation(kpre[:, :], pband[:, :], AF.Copy,
                                              scale=QSCALE).then_inc(ev, 1)
                                sc.wait_ge(ev, c["ev"] + 1)
                                sc.copy(swstg[0:64, :], kpre[64:128, :]).then_inc(ev, 1)
                                sc.copy(swstg[64:128, :], kpre[0:64, :]).then_inc(ev, 1)
                            c["ev"] += 3

                            @block.vector
                            def _(v, oc=oc, w=w):
                                v.wait_ge(ev, c["ev"])
                                v.wait_ge(glt, 32)
                                v.tensor_tensor(rstgA[:, :], kpre[:, :],
                                                ropeC[:, 512 * w:512 * w + 512],
                                                ALU.mult).then_inc(vx, 1)
                                v.tensor_tensor(rstgB[:, :], swstg[:, :],
                                                ropeS[:, 512 * w:512 * w + 512],
                                                ALU.mult).then_inc(vx, 1)
                                v.wait_ge(vx, c["vx"] + 2)
                                v.tensor_tensor(qw[:, oc, :], rstgA[:, :],
                                                rstgB[:, :], ALU.add).then_inc(vx, 1)
                            c["vx"] += 3

                for h in range(NC8):
                    @block.tensor
                    def _(t, h=h):
                        t.wait_ge(ld, 80)
                        t.wait_ge(vx, c["vx"])
                        t.wait_ge(ev, c["ev"])
                        t.matmul(pb2[:, :], kvbf[:, h, 0:128], qw[:, h, :],
                                 start=True, stop=False)
                        t.matmul(pb2[:, :], vsbf[0:1, h, :], ones512[:, :],
                                 start=False, stop=True)
                        t.matmul(pb3[0:1, :], kvbf[:, h, 128:129], qw[:, h, :],
                                 start=True, stop=True).then_inc(mm, 1)
                    c["mm"] += 1

                    @block.scalar
                    def _(sc, h=h):
                        sc.wait_ge(mm, c["mm"])
                        sc.wait_ge(vx, c["vx"])
                        sc.activation(denr[:, :], pb3[0:1, :], AF.Copy,
                                      bias=2048.0).then_inc(ev, 1)
                        sc.copy(numst[:, :], pb2[:, :]).then_inc(ev, 1)
                    c["ev"] += 2

                    @block.vector
                    def _(v, h=h):
                        v.wait_ge(ev, c["ev"])
                        v.reciprocal(denri[:, :], denr[:, :]).then_inc(vx, 1)
                    c["vx"] += 1

                    @block.tensor
                    def _(t, h=h):
                        t.wait_ge(vx, c["vx"])
                        t.matmul(pb4[:, :], ones_rf[:, :], denri[:, :],
                                 start=True, stop=True).then_inc(mm, 1)
                    c["mm"] += 1

                    @block.vector
                    def _(v, h=h):
                        v.wait_ge(mm, c["mm"])
                        v.tensor_tensor(osb[:, h, :], numst[:, :], pb4[:, :],
                                        ALU.mult).then_inc(vx, 1)
                    c["vx"] += 1

                @block.vector
                def _(v, w=w, s=s):
                    v.wait_ge(ld, 80)
                    v.wait_ge(stS, c["stS"])
                    v.wait_ge(stG, c["stG"])
                    v.wait_ge(ev, c["ev"])
                    base = 512 * w
                    for cc in range(NC8):
                        ci = v.tensor_scalar(lepacc[:, cc, :],
                                             vfm[:, cc, base + 0:base + 512],
                                             lepe_wc[:, cc, 0:1], lepe_bc[:, cc:cc + 1],
                                             ALU.mult, ALU.add)
                    ci.then_inc(vx, 1)
                    for tap in range(1, 5):
                        v.wait_ge(vx, c["vx"] + tap)
                        for cc in range(NC8):
                            ci = v.scalar_tensor_tensor(lepacc[:, cc, :],
                                                        vfm[:, cc, base + tap:base + tap + 512],
                                                        lepe_wc[:, cc, tap:tap + 1],
                                                        lepacc[:, cc, :],
                                                        ALU.mult, ALU.add)
                        ci.then_inc(vx, 1)
                c["vx"] += 5

                @block.vector
                def _(v):
                    v.wait_ge(vx, c["vx"])
                    for cc in range(NC8):
                        ci = v.tensor_tensor(osb[:, cc, :], osb[:, cc, :],
                                             lepacc[:, cc, :], ALU.add)
                    ci.then_inc(vx, 1)
                    v.wait_ge(vx, c["vx"] + 1)
                    for cc in range(NC8):
                        ci = v.tensor_tensor(obf[:, cc, :], osb[:, cc, :],
                                             gw[:, cc, :], ALU.mult)
                    ci.then_inc(vx, 1)
                c["vx"] += 2

                for oc in range(NC8):
                    b = oc % 2

                    @block.gpsimd
                    def _(g_, oc=oc, b=b):
                        g_.wait_ge(mm, wsl_free_mm[b])
                        g_.dma_start(out=wsl[:, b, :, :],
                                     in_=wo[:, :, 128 * oc:128 * oc + 128]
                                     .rearrange("c p o -> p c o")).then_inc(GL[b], 16)
                    c["gl%d" % b] += 16

                    @block.tensor
                    def _(t, oc=oc, b=b):
                        t.wait_ge(GL[b], c["gl%d" % b])
                        t.wait_ge(vx, c["vx"])
                        for cc in range(NC8):
                            mi = t.matmul(pb5[:, :], wsl[:, b, cc, :], obf[:, cc, :],
                                          start=(cc == 0), stop=(cc == NC8 - 1))
                            if cc == NC8 - 1:
                                mi.then_inc(mm, 1)
                    wsl_free_mm[b] = c["mm"] + 1
                    c["mm"] += 1

                    @block.vector
                    def _(v, oc=oc):
                        v.wait_ge(mm, c["mm"])
                        v.wait_ge(ld, c["ld"])
                        v.tensor_tensor(zpre[:, oc, :], pb5[:, :],
                                        xres[:, oc, :], ALU.add).then_inc(vx, 1)
                    c["vx"] += 1

                emit_ln_stripe(block, B, c, col0, z1_h, z1b_h)

        @block.sync
        def _(sy):
            sy.wait_ge(stS, c["stS"])
            sy.wait_ge(stG, c["stG"])
    return nc



def build_launch_b():
    nc = bass.Bass("TRN2", target_bir_lowering=False)
    z1_h = nc.dram_tensor("z1_fm", [NC8, 128, T], F32, kind="ExternalInput").ap()
    z1b_h = nc.dram_tensor("z1_bf", [NC8, 128, T], BF16, kind="ExternalInput").ap()
    gw_h = nc.dram_tensor("gate_w", [NC8, 128, 8], F32, kind="ExternalInput").ap()
    gb_h = nc.dram_tensor("gate_b", [1, 8], F32, kind="ExternalInput").ap()
    w1_h = nc.dram_tensor("w1_t", [8, NC8, 128, 4096], F32, kind="ExternalInput").ap()
    b1_h = nc.dram_tensor("b1_r", [8, 1, 4096], F32, kind="ExternalInput").ap()
    w2_h = nc.dram_tensor("w2_t", [8, 32, 128, D], F32, kind="ExternalInput").ap()
    b2_h = nc.dram_tensor("b2_r", [8, D], F32, kind="ExternalInput").ap()
    sel_h = nc.dram_tensor("sel_bf", [8, 8, 128], BF16, kind="ExternalInput").ap()
    iden_h = nc.dram_tensor("iden_bf", [128, 128], BF16, kind="ExternalInput").ap()
    onc_h = nc.dram_tensor("ones_col_bf", [128, 1], BF16, kind="ExternalInput").ap()
    onrf_h = nc.dram_tensor("ones_row_f", [1, 128], F32, kind="ExternalInput").ap()
    acc_h = nc.dram_tensor("acc_b", [NC8, 128, T], F32, kind="ExternalOutput").ap()
    z2_h = nc.dram_tensor("z2_fm", [NC8, 128, T], F32, kind="ExternalOutput").ap()
    z2b_h = nc.dram_tensor("z2_bf", [NC8, 128, T], BF16, kind="ExternalOutput").ap()

    import contextlib
    es = contextlib.ExitStack()
    def sb(name, shape, dtype):
        return es.enter_context(nc.sbuf_tensor(name, shape, dtype))
    def psf(name):
        return es.enter_context(nc.psum_tensor(name, [128, 512], F32))
    def sem(name):
        return es.enter_context(nc.semaphore(name))
    z1bf = sb("z1bf", [128, NC8, T], BF16)
    z1res = sb("z1res", [128, NC8, 512], F32)
    w1q = sb("w1q", [128, 2, NC8, 1024], BF16)
    w2q = sb("w2q", [128, 2, NC8, 1024], BF16)
    hbf = sb("hbf", [128, NC8, 512], BF16)
    ysb = sb("ysb", [128, NC8, 512], F32)
    s8 = sb("s8", [8, NC8, 512], BF16)
    b1sb = sb("b1sb", [1, 1024], BF16)
    sbc_sb = sb("sbc_sb", [128, 512], BF16)
    b2sb = sb("b2sb", [8, D], BF16)
    selsb = sb("selsb", [8, 8, 128], BF16)
    gwsb = sb("gwsb", [128, NC8, 8], F32)
    gbsb = sb("gbsb", [1, 8], F32)
    msb = sb("msb", [128, 1], F32)
    shsb = sb("shsb", [128, 8], F32)
    esb = sb("esb", [128, 8], F32)
    ssum = sb("ssum", [128, 1], F32)
    psb = sb("psb", [128, 1], F32)
    msk = sb("msk", [128, 8], F32)
    scb = sb("scb", [128, 8], BF16)
    iden = sb("iden_sb", [128, 128], BF16)
    ones_col = sb("ones_col_sb", [128, 1], BF16)
    ones_rf = sb("ones_row_sb", [1, 128], F32)
    ones512 = sb("ones512", [1, 512], BF16)
    epsc = sb("epsc", [1, 1], F32)
    zbf = hbf
    zsq = w1q[:, 0, :, 0:512]
    row_mean = sb("row_mean", [1, 512], F32)
    row_ms = sb("row_ms", [1, 512], F32)
    row_var = sb("row_var", [1, 512], F32)
    row_r = sb("row_r", [1, 512], F32)
    row_tmp = sb("row_tmp", [1, 512], F32)
    pbG, pbT, pbB = psf("pbG"), psf("pbT"), psf("pbB")
    pbH0, pbH1, pbY0, pbY1 = psf("pbH0"), psf("pbH1"), psf("pbY0"), psf("pbY1")
    pbX = psf("pbX")
    ld, mm, ev, vx = sem("ld"), sem("mm"), sem("ev"), sem("vx")
    lr = sem("lr")
    glA, glB, glC, glD, glE = sem("glA"), sem("glB"), sem("glC"), sem("glD"), sem("glE")
    stS, stG = sem("stS"), sem("stG")

    with es, nc.Block() as block:
        c = dict(ld=0, lr=0, mm=0, ev=0, vx=0, glA=0, glB=0, glC=0, glD=0,
                 glE=0, stS=0, stG=0)
        W1S = [(glA, "glA"), (glB, "glB")]
        W2S = [(glC, "glC"), (glD, "glD")]
        pbTb = pbT.bitcast(BF16)
        NW = T // 512

        @block.sync
        def _(sy):
            sy.dma_start(out=z1bf[:, :, 0:2048],
                         in_=z1b_h[:, :, 0:2048]
                         .rearrange("c p t -> p c t")).then_inc(ld, 16)
            sy.dma_start(out=z1bf[:, :, 2048:4096],
                         in_=z1b_h[:, :, 2048:4096]
                         .rearrange("c p t -> p c t")).then_inc(ld, 16)
            sy.dma_start(out=iden[:, :], in_=iden_h).then_inc(ld, 16)
            sy.dma_start(out=ones_col[:, :], in_=onc_h).then_inc(ld, 16)
            sy.dma_start(out=ones_rf[:, :], in_=onrf_h).then_inc(ld, 16)
            sy.dma_start(out=gwsb[:, :, :],
                         in_=gw_h.rearrange("c p e -> p c e")).then_inc(ld, 16)
            sy.dma_start(out=gbsb[:, :], in_=gb_h).then_inc(ld, 16)
            sy.dma_start(out=selsb[:, :, :], in_=sel_h).then_inc(ld, 16)
        c["ld"] += 128
        LD_STATIC = c["ld"]

        @block.gpsimd
        def _(g):
            g.dma_start(out=b2sb[:, :], in_=b2_h).then_inc(glE, 16)
        c["glE"] += 16

        @block.vector
        def _(v):
            v.memset(ones512[:, :], 1.0)
            v.memset(epsc[:, :], EPS).then_inc(vx, 1)
        c["vx"] += 1

        B = dict(zpre=ysb, zbf=zbf, zsq=zsq, row_mean=row_mean, row_ms=row_ms,
                 row_var=row_var, row_r=row_r, row_tmp=row_tmp,
                 ones_col_bf=ones_col, ones_row_f=ones_rf, epsc=epsc,
                 pS=pbG, pQ=pbT, pR=pbB, pM=pbX, ld_static=LD_STATIC,
                 s_ld=ld, s_mm=mm, s_ev=ev, s_vx=vx, s_stS=stS, s_stG=stG)

        for w in range(NW):
            @block.sync
            def _(sy, w=w):
                if w >= 1:
                    sy.wait_ge(mm, c["mm"])
                sy.dma_start(out=z1res[:, :, :],
                             in_=z1_h[:, :, 512 * w:512 * w + 512]
                             .rearrange("c p t -> p c t")).then_inc(lr, 16)
            c["lr"] += 16

            for u in range(4):
                @block.tensor
                def _(t, w=w, u=u):
                    t.wait_ge(ld, LD_STATIC)
                    t.wait_ge(lr, c["lr"])
                    if u == 0 and w >= 1:
                        t.wait_ge(vx, c["vx"])
                    for cc in range(NC8):
                        t.matmul(pbG[:, 8 * u:8 * u + 8],
                                 z1res[:, cc, 128 * u:128 * u + 128],
                                 gwsb[:, cc, :],
                                 start=(cc == 0), stop=False)
                    t.matmul(pbG[:, 8 * u:8 * u + 8], ones_rf[:, :], gbsb[:, :],
                             start=False, stop=True).then_inc(mm, 1)
                c["mm"] += 1

                @block.vector
                def _(v, w=w, u=u):
                    v.wait_ge(mm, c["mm"])
                    v.reduce_max(msb[:, :], pbG[:, 8 * u:8 * u + 8],
                                 axis=AX.X).then_inc(vx, 1)
                    v.wait_ge(vx, c["vx"] + 1)
                    v.tensor_scalar(shsb[:, :], pbG[:, 8 * u:8 * u + 8],
                                    msb[:, :], None, ALU.subtract).then_inc(vx, 1)
                    v.wait_ge(vx, c["vx"] + 2)
                    v.tensor_scalar(msk[:, :], pbG[:, 8 * u:8 * u + 8],
                                    msb[:, :], None, ALU.is_ge).then_inc(vx, 1)
                c["vx"] += 3

                @block.scalar
                def _(sc, u=u):
                    sc.wait_ge(vx, c["vx"])
                    sc.activation(esb[:, :], shsb[:, :], AF.Exp).then_inc(ev, 1)
                c["ev"] += 1

                @block.vector
                def _(v, u=u):
                    v.wait_ge(ev, c["ev"])
                    v.reduce_sum(ssum[:, :], esb[:, :], axis=AX.X).then_inc(vx, 1)
                    v.wait_ge(vx, c["vx"] + 1)
                    v.reciprocal(psb[:, :], ssum[:, :]).then_inc(vx, 1)
                    v.wait_ge(vx, c["vx"] + 2)
                    v.tensor_scalar(scb[:, :], msk[:, :], psb[:, :], None,
                                    ALU.mult).then_inc(vx, 1)
                c["vx"] += 3

                @block.tensor
                def _(t, w=w, u=u):
                    t.wait_ge(vx, c["vx"])
                    if not (w == 0 and u == 0):
                        t.wait_ge(ev, c["ev"])
                    t.transpose(pbTb[0:8, 0:128], scb[:, :], iden[:, :]).then_inc(mm, 1)
                c["mm"] += 1

                @block.scalar
                def _(sc, w=w, u=u):
                    sc.wait_ge(mm, c["mm"])
                    sc.copy(s8[:, w, 128 * u:128 * u + 128],
                            pbTb[0:8, 0:128]).then_inc(ev, 1)
                c["ev"] += 1

        b1_free_mm = 0
        w1_free_mm = [0, 0]
        w2_free_mm = [0, 0]
        b1_free_ev = 0
        for e in range(8):
            for q in range(4):
                slot = q % 2

                @block.gpsimd
                def _(g, e=e, q=q):
                    g.wait_ge(mm, b1_free_mm)
                    g.dma_start(out=b1sb[:, :],
                                in_=b1_h[e, :, 1024 * q:1024 * q + 1024]
                                ).then_inc(glE, 16)
                c["glE"] += 16

                @block.gpsimd
                def _(g, e=e, q=q, slot=slot):
                    g.wait_ge(mm, w1_free_mm[slot])
                    g.dma_start(out=w1q[:, slot, :, :],
                                in_=w1_h[e, :, :, 1024 * q:1024 * q + 1024]
                                .rearrange("c p m -> p c m")).then_inc(W1S[slot][0], 16)
                c[W1S[slot][1]] += 16

                @block.gpsimd
                def _(g, e=e, q=q, slot=slot):
                    g.wait_ge(mm, w2_free_mm[slot])
                    g.dma_start(out=w2q[:, slot, :, :],
                                in_=w2_h[e, 8 * q:8 * q + 8, :, :]
                                .rearrange("m p d -> p m d")).then_inc(W2S[slot][0], 16)
                c[W2S[slot][1]] += 16

                for w in range(NW):
                    hev_base = c["ev"]
                    for mc in range(NC8):
                        pbh = pbH0 if (mc % 2 == 0) else pbH1

                        @block.tensor
                        def _(t, e=e, q=q, w=w, mc=mc, slot=slot, pbh=pbh,
                              hev_base=hev_base):
                            t.wait_ge(W1S[slot][0], c[W1S[slot][1]])
                            t.wait_ge(glE, c["glE"])
                            t.wait_ge(ld, LD_STATIC)
                            if mc < 2:
                                t.wait_ge(mm, c["mm"])
                                t.wait_ge(ev, c["ev"])
                            else:
                                t.wait_ge(ev, hev_base + (mc - 2) + 1)
                            for cc in range(NC8):
                                t.matmul(pbh[:, :], w1q[:, slot, cc, 128 * mc:128 * mc + 128],
                                         z1bf[:, cc, 512 * w:512 * w + 512],
                                         start=(cc == 0), stop=False)
                            t.matmul(pbh[:, :],
                                     b1sb[0:1, 128 * mc:128 * mc + 128],
                                     ones512[:, :], start=False,
                                     stop=True).then_inc(mm, 1)
                        c["mm"] += 1

                        @block.scalar
                        def _(sc, mc=mc, pbh=pbh):
                            sc.wait_ge(mm, c["mm"])
                            sc.activation(hbf[:, mc, :], pbh[:, :],
                                          AF.Relu).then_inc(ev, 1)
                        c["ev"] += 1

                    @block.tensor
                    def _(t, e=e, w=w):
                        t.wait_ge(ev, c["ev"])
                        t.wait_ge(vx, c["vx"])
                        t.matmul(pbB[:, :], selsb[:, e, :], s8[:, w, :],
                                 start=True, stop=True).then_inc(mm, 1)
                    c["mm"] += 1

                    @block.scalar
                    def _(sc):
                        sc.wait_ge(mm, c["mm"])
                        sc.copy(sbc_sb[:, :], pbB[:, :]).then_inc(ev, 1)
                    c["ev"] += 1

                    if e == 0 and (q == 0 or q == 3):
                        @block.sync
                        def _(sy, w=w):
                            sy.wait_ge(vx, c["vx"])
                            sy.dma_start(out=z1res[:, :, :],
                                         in_=z1_h[:, :, 512 * w:512 * w + 512]
                                         .rearrange("c p t -> p c t")
                                         ).then_inc(lr, 16)
                        c["lr"] += 16

                    for dc in range(NC8):
                        pby = pbY0 if (dc % 2 == 0) else pbY1

                        @block.tensor
                        def _(t, e=e, q=q, w=w, dc=dc, slot=slot, pby=pby):
                            t.wait_ge(W2S[slot][0], c[W2S[slot][1]])
                            t.wait_ge(ev, c["ev"])
                            t.wait_ge(vx, c["vx"])
                            for mc in range(NC8):
                                mi = t.matmul(pby[:, :],
                                              w2q[:, slot, mc, 128 * dc:128 * dc + 128],
                                              hbf[:, mc, :],
                                              start=(mc == 0), stop=(mc == NC8 - 1))
                            mi.then_inc(mm, 1)
                        c["mm"] += 1

                        @block.vector
                        def _(v, e=e, q=q, w=w, dc=dc, pby=pby):
                            v.wait_ge(mm, c["mm"])
                            v.wait_ge(ev, c["ev"])
                            v.wait_ge(stG, c["stG"])
                            if e == 0 and q == 0:
                                v.wait_ge(lr, c["lr"])
                                v.tensor_tensor(ysb[:, dc, :], pby[:, :],
                                                sbc_sb[:, :], ALU.mult).then_inc(vx, 1)
                                v.wait_ge(vx, c["vx"] + 1)
                                v.tensor_tensor(ysb[:, dc, :], ysb[:, dc, :],
                                                z1res[:, dc, :], ALU.add).then_inc(vx, 1)
                            else:
                                v.tensor_tensor(ysb[:, dc, :], pby[:, :],
                                                sbc_sb[:, :], ALU.mult).then_inc(vx, 1)
                        c["vx"] += 2 if (e == 0 and q == 0) else 1

                    if e == 0 and q == 3:
                        for dc in range(NC8):
                            pby = pbY0 if (dc % 2 == 0) else pbY1

                            @block.tensor
                            def _(t, w=w, dc=dc, pby=pby):
                                t.wait_ge(glE, c["glE"])
                                t.wait_ge(vx, c["vx"])
                                t.matmul(pby[:, :], b2sb[:, 128 * dc:128 * dc + 128],
                                         s8[:, w, :], start=True,
                                         stop=True).then_inc(mm, 1)
                            c["mm"] += 1

                            @block.vector
                            def _(v, w=w, dc=dc, pby=pby):
                                v.wait_ge(mm, c["mm"])
                                v.wait_ge(lr, c["lr"])
                                v.tensor_tensor(ysb[:, dc, :], ysb[:, dc, :],
                                                pby[:, :], ALU.add).then_inc(vx, 1)
                            c["vx"] += 1

                    @block.gpsimd
                    def _(g, e=e, q=q, w=w):
                        g.wait_ge(vx, c["vx"])
                        g.wait_ge(stG, c["stG"])
                        if e == 0 and q == 0:
                            g.dma_start(out=acc_h[:, :, 512 * w:512 * w + 512]
                                        .rearrange("c p t -> p c t"),
                                        in_=ysb[:, :, :]).then_inc(stG, 16)
                        else:
                            g.dma_start(out=acc_h[:, :, 512 * w:512 * w + 512]
                                        .rearrange("c p t -> p c t"),
                                        in_=ysb[:, :, :],
                                        accum_op=ALU.add).then_inc(stG, 16)
                    c["stG"] += 16

                w1_free_mm[slot] = c["mm"]
                w2_free_mm[slot] = c["mm"]
                b1_free_mm = c["mm"]

        for w in range(NW):
            @block.sync
            def _(sy, w=w):
                sy.wait_ge(stG, c["stG"])
                if w >= 1:
                    sy.wait_ge(stS, c["stS"])
                    sy.wait_ge(mm, c["mm"])
                sy.dma_start(out=ysb[:, :, :],
                             in_=acc_h[:, :, 512 * w:512 * w + 512]
                             .rearrange("c p t -> p c t")).then_inc(lr, 16)
            c["lr"] += 16
            B["zpre_ready"] = (lr, c["lr"])
            emit_ln_stripe(block, B, c, 512 * w, z2_h, z2b_h)

        @block.sync
        def _(sy):
            sy.wait_ge(stS, c["stS"])
            sy.wait_ge(stG, c["stG"])
    return nc



SC1 = float(np.float32(1.0) / np.sqrt(np.float32(16 + 512 + 512)))


def build_launch_c():
    nc = bass.Bass("TRN2", target_bir_lowering=False)
    z2_h = nc.dram_tensor("z2_fm", [NC8, 128, T], F32, kind="ExternalInput").ap()
    z2b_h = nc.dram_tensor("z2_bf", [NC8, 128, T], BF16, kind="ExternalInput").ap()
    wdq_h = nc.dram_tensor("wdq_t", [NC8, 128, 512], F32, kind="ExternalInput").ap()
    bdq_h = nc.dram_tensor("bdq_c", [128, 4], F32, kind="ExternalInput").ap()
    wuq_h = nc.dram_tensor("wuq_t", [4, 128, D], F32, kind="ExternalInput").ap()
    buq_h = nc.dram_tensor("buq_c", [128, NC8], F32, kind="ExternalInput").ap()
    wqr_h = nc.dram_tensor("wqr_t", [4, 128, D], F32, kind="ExternalInput").ap()
    bqr_h = nc.dram_tensor("bqr_c", [128, NC8], F32, kind="ExternalInput").ap()
    wkr_h = nc.dram_tensor("wkr_t", [NC8, 128, D], F32, kind="ExternalInput").ap()
    bkr_h = nc.dram_tensor("bkr_c", [128, NC8], F32, kind="ExternalInput").ap()
    wdkv_h = nc.dram_tensor("wdkv_t", [NC8, 128, 512], F32, kind="ExternalInput").ap()
    bdkv_h = nc.dram_tensor("bdkv_c", [128, 4], F32, kind="ExternalInput").ap()
    wuk_h = nc.dram_tensor("wuk_t", [4, 128, D], F32, kind="ExternalInput").ap()
    buk_h = nc.dram_tensor("buk_c", [128, NC8], F32, kind="ExternalInput").ap()
    wuv_h = nc.dram_tensor("wuv_t", [4, 128, D], F32, kind="ExternalInput").ap()
    buv_h = nc.dram_tensor("buv_r", [1, D], F32, kind="ExternalInput").ap()
    wo1_h = nc.dram_tensor("wo1_t", [NC8, 128, D], F32, kind="ExternalInput").ap()
    wob_h = nc.dram_tensor("wob_c", [128, NC8], F32, kind="ExternalInput").ap()
    rc_h = nc.dram_tensor("l1_ropec", [NC8, 128, S], F32, kind="ExternalInput").ap()
    rs_h = nc.dram_tensor("l1_ropes", [NC8, 128, S], F32, kind="ExternalInput").ap()
    iden_h = nc.dram_tensor("iden_bf", [128, 128], BF16, kind="ExternalInput").ap()
    onc_h = nc.dram_tensor("ones_col_bf", [128, 1], BF16, kind="ExternalInput").ap()
    onrb_h = nc.dram_tensor("ones_row_bf", [1, 128], BF16, kind="ExternalInput").ap()
    onrf_h = nc.dram_tensor("ones_row_f", [1, 128], F32, kind="ExternalInput").ap()
    z3_h = nc.dram_tensor("z3_fm", [NC8, 128, T], F32, kind="ExternalOutput").ap()
    z3b_h = nc.dram_tensor("z3_bf", [NC8, 128, T], BF16, kind="ExternalOutput").ap()

    import contextlib
    es = contextlib.ExitStack()
    def sb(name, shape, dtype):
        return es.enter_context(nc.sbuf_tensor(name, shape, dtype))
    def psf(name):
        return es.enter_context(nc.psum_tensor(name, [128, 512], F32))
    def sem(name):
        return es.enter_context(nc.semaphore(name))
    z2bf = sb("z2bf", [128, NC8, S], BF16)
    ctq = sb("ctq", [128, 4, S], BF16)
    ctkv = sb("ctkv", [128, 4, S], BF16)
    kbf = sb("kbf", [128, NC8, S], BF16)
    vaug = sb("vaug", [128, 16, 1040], BF16)
    wsl = sb("wsl", [128, 2, NC8, 128], BF16)
    wuvsb = sb("wuvsb", [128, 4, D], BF16)
    buvsb = sb("buvsb", [1, D], BF16)
    tblC = sb("tblC", [128, 512], F32)
    tblS = sb("tblS", [128, 512], F32)
    rstgA = sb("rstgA", [128, 512], BF16)
    rstgB = sb("rstgB", [128, 512], BF16)
    swstg = sb("swstg", [128, 512], BF16)
    pstg = sb("pstg", [128, 512], BF16)
    ktc_st = sb("ktc_st", [128, 512], BF16)
    numst = sb("numst", [64, 512], F32)
    denr = sb("denr", [1, 512], F32)
    denri = sb("denri", [1, 512], F32)
    obf = sb("obf", [128, NC8, 512], BF16)
    wo_st = sb("wo_st", [128, 512], F32)
    bdq = sb("bdq", [128, 4], F32)
    buq = sb("buq", [128, NC8], F32)
    bqr = sb("bqr", [128, NC8], F32)
    bkr = sb("bkr", [128, NC8], F32)
    bdkv = sb("bdkv", [128, 4], F32)
    buk = sb("buk", [128, NC8], F32)
    wob = sb("wob", [128, NC8], F32)
    iden = sb("iden_sb", [128, 128], BF16)
    ones_col = sb("ones_col_sb", [128, 1], BF16)
    ones_rb = sb("ones_row_b_sb", [1, 128], BF16)
    ones_rf = sb("ones_row_sb", [1, 128], F32)
    epsc = sb("epsc", [1, 1], F32)
    zpre = sb("zpre", [128, NC8, 512], F32)
    zsq = sb("zsq", [128, NC8, 512], BF16)
    row_mean = sb("row_mean", [1, 512], F32)
    row_ms = sb("row_ms", [1, 512], F32)
    row_var = sb("row_var", [1, 512], F32)
    row_r = sb("row_r", [1, 512], F32)
    row_tmp = sb("row_tmp", [1, 512], F32)
    pb0, pb1, pb2, pb3 = psf("pb0"), psf("pb1"), psf("pb2"), psf("pb3")
    pb4, pb5, pb6, pb7 = psf("pb4"), psf("pb5"), psf("pb6"), psf("pb7")
    ld, lr, mm, ev, vx = sem("ld"), sem("lr"), sem("mm"), sem("ev"), sem("vx")
    glx, gl0, gl1, glt = sem("glx"), sem("gl0"), sem("gl1"), sem("glt")
    stS, stG = sem("stS"), sem("stG")

    with es, nc.Block() as block:
        c = dict(ld=0, lr=0, mm=0, ev=0, vx=0, glx=0, gl0=0, gl1=0, glt=0,
                 stS=0, stG=0)
        GL = (gl0, gl1)
        GLN = ("gl0", "gl1")

        def E_ap(kt):
            return z2bf[:, (512 * kt) // S, (512 * kt) % S:(512 * kt) % S + 512]

        def q_ap(oc, p0=0, pn=128):
            base = 8192 + 512 * oc
            return z2bf[p0:p0 + pn, base // S, base % S:base % S + 512]

        @block.sync
        def _(sy):
            sy.dma_start(out=iden[:, :], in_=iden_h).then_inc(ld, 16)
            sy.dma_start(out=ones_col[:, :], in_=onc_h).then_inc(ld, 16)
            sy.dma_start(out=ones_rb[:, :], in_=onrb_h).then_inc(ld, 16)
            sy.dma_start(out=ones_rf[:, :], in_=onrf_h).then_inc(ld, 16)
            sy.dma_start(out=bdq[:, :], in_=bdq_h).then_inc(ld, 16)
            sy.dma_start(out=buq[:, :], in_=buq_h).then_inc(ld, 16)
            sy.dma_start(out=bqr[:, :], in_=bqr_h).then_inc(ld, 16)
            sy.dma_start(out=bkr[:, :], in_=bkr_h).then_inc(ld, 16)
            sy.dma_start(out=bdkv[:, :], in_=bdkv_h).then_inc(ld, 16)
            sy.dma_start(out=buk[:, :], in_=buk_h).then_inc(ld, 16)
            sy.dma_start(out=wob[:, :], in_=wob_h).then_inc(ld, 16)
        c["ld"] += 176
        LD_STATIC = c["ld"]

        @block.gpsimd
        def _(g):
            g.dma_start(out=wuvsb[:, :, :],
                        in_=wuv_h.rearrange("c p o -> p c o")).then_inc(glt, 16)
            g.dma_start(out=buvsb[:, :], in_=buv_h).then_inc(glt, 16)
        c["glt"] += 32

        @block.vector
        def _(v):
            v.memset(epsc[:, :], EPS)
            for j in range(16):
                for h in range(16):
                    mi = v.memset(vaug[:, j, 65 * h + 64:65 * h + 65], 1.0)
            mi.then_inc(vx, 1)
        c["vx"] += 1

        B = dict(zpre=zpre, zbf=obf, zsq=zsq, row_mean=row_mean, row_ms=row_ms,
                 row_var=row_var, row_r=row_r, row_tmp=row_tmp,
                 ones_col_bf=ones_col, ones_row_f=ones_rf, epsc=epsc,
                 pS=pb6, pQ=pb7, pR=pb4, pM=pb5, ld_static=LD_STATIC,
                 s_ld=ld, s_mm=mm, s_ev=ev, s_vx=vx, s_stS=stS, s_stG=stG)

        wsl_free_mm = [0, 0]

        def load_wslice(src_ap, oc, b, nchunk=NC8):
            @block.gpsimd
            def _(g, src_ap=src_ap, oc=oc, b=b, nchunk=nchunk):
                g.wait_ge(mm, wsl_free_mm[b])
                g.dma_start(out=wsl[:, b, 0:nchunk, :],
                            in_=src_ap[:, :, 128 * oc:128 * oc + 128]
                            .rearrange("c p o -> p c o")).then_inc(GL[b], 16)
            c[GLN[b]] += 16

        def rope_slice(srcpre, tbl_oc, w, dst_ap, addsrc):
            @block.scalar
            def _(sc):
                sc.wait_ge(ev, c["ev"])
                sc.wait_ge(vx, c["vx"])
                sc.wait_ge(lr, c["lr"])
                sc.copy(swstg[0:32, :], srcpre[32:64, :]).then_inc(ev, 1)
                sc.copy(swstg[32:64, :], srcpre[0:32, :]).then_inc(ev, 1)
                sc.copy(swstg[64:96, :], srcpre[96:128, :]).then_inc(ev, 1)
                sc.copy(swstg[96:128, :], srcpre[64:96, :]).then_inc(ev, 1)
            c["ev"] += 4

            @block.vector
            def _(v):
                v.wait_ge(ev, c["ev"])
                v.tensor_tensor(rstgA[:, :], srcpre, tblC[:, :], ALU.mult).then_inc(vx, 1)
                v.tensor_tensor(rstgB[:, :], swstg[:, :], tblS[:, :], ALU.mult).then_inc(vx, 1)
                v.wait_ge(vx, c["vx"] + 2)
                if addsrc is None:
                    v.tensor_tensor(dst_ap, rstgA[:, :], rstgB[:, :],
                                    ALU.add).then_inc(vx, 1)
                else:
                    v.tensor_tensor(rstgA[:, :], rstgA[:, :], rstgB[:, :],
                                    ALU.add).then_inc(vx, 1)
                    v.wait_ge(vx, c["vx"] + 3)
                    v.tensor_tensor(dst_ap, rstgA[:, :], addsrc,
                                    ALU.add).then_inc(vx, 1)
            c["vx"] += 3 if addsrc is None else 4

        for s in range(NSEQ):
            @block.gpsimd
            def _(g, s=s):
                if s == 1:
                    g.wait_ge(mm, c["mm"])
                g.dma_start(out=z2bf[:, :, :],
                            in_=z2b_h[:, :, s * S:(s + 1) * S]
                            .rearrange("c p t -> p c t")).then_inc(glx, 16)
            c["glx"] += 16

            for (wsrc, dst, bias, no) in ((wdq_h, ctq, bdq, 4), (wdkv_h, ctkv, bdkv, 4)):
                for oc in range(no):
                    b = oc % 2
                    load_wslice(wsrc, oc, b)
                    ev_base = c["ev"]

                    @block.tensor
                    def _(t, oc=oc, b=b, ev_base=ev_base):
                        t.wait_ge(GL[b], c[GLN[b]])
                        t.wait_ge(glx, c["glx"])
                        t.wait_ge(ev, ev_base)
                        for w in range(4):
                            pband = pb0 if (w % 2 == 0) else pb1
                            if w >= 2:
                                t.wait_ge(ev, ev_base + (w - 2) + 1)
                            for cc in range(NC8):
                                mi = t.matmul(pband[:, :], wsl[:, b, cc, :],
                                              z2bf[:, cc, 512 * w:512 * w + 512],
                                              start=(cc == 0), stop=(cc == NC8 - 1))
                            mi.then_inc(mm, 1)
                    wsl_free_mm[b] = c["mm"] + 4
                    c["mm"] += 4

                    @block.scalar
                    def _(sc, oc=oc, dst=dst, bias=bias):
                        for w in range(4):
                            pband = pb0 if (w % 2 == 0) else pb1
                            sc.wait_ge(mm, c["mm"] - 3 + w)
                            sc.wait_ge(ld, LD_STATIC)
                            sc.activation(dst[:, oc, 512 * w:512 * w + 512],
                                          pband[:, :], AF.Identity,
                                          bias=bias[:, oc:oc + 1]).then_inc(ev, 1)
                    c["ev"] += 4

            for oc in range(NC8):
                b = oc % 2
                load_wslice(wkr_h, oc, b)
                ev_base = c["ev"]

                @block.tensor
                def _(t, oc=oc, b=b, ev_base=ev_base):
                    t.wait_ge(GL[b], c[GLN[b]])
                    t.wait_ge(glx, c["glx"])
                    t.wait_ge(ev, ev_base)
                    for w in range(4):
                        pband = pb0 if (w % 2 == 0) else pb1
                        if w >= 2:
                            t.wait_ge(ev, ev_base + (w - 2) + 1)
                        for cc in range(NC8):
                            mi = t.matmul(pband[:, :], wsl[:, b, cc, :],
                                          z2bf[:, cc, 512 * w:512 * w + 512],
                                          start=(cc == 0), stop=(cc == NC8 - 1))
                        mi.then_inc(mm, 1)
                wsl_free_mm[b] = c["mm"] + 4
                c["mm"] += 4

                @block.scalar
                def _(sc, oc=oc):
                    for w in range(4):
                        pband = pb0 if (w % 2 == 0) else pb1
                        sc.wait_ge(mm, c["mm"] - 3 + w)
                        sc.wait_ge(ld, LD_STATIC)
                        sc.activation(kbf[:, oc, 512 * w:512 * w + 512],
                                      pband[:, :], AF.Identity,
                                      bias=bkr[:, oc:oc + 1]).then_inc(ev, 1)
                c["ev"] += 4

            for oc in range(NC8):
                b = oc % 2
                load_wslice(wuk_h, oc, b, nchunk=4)
                for w in range(4):
                    @block.sync
                    def _(sy, oc=oc, w=w, s=s):
                        sy.wait_ge(vx, c["vx"])
                        sy.dma_start(out=tblC[:, :],
                                     in_=rc_h[oc, :, 512 * w:512 * w + 512]).then_inc(lr, 16)
                        sy.dma_start(out=tblS[:, :],
                                     in_=rs_h[oc, :, 512 * w:512 * w + 512]).then_inc(lr, 16)
                    c["lr"] += 32

                    @block.tensor
                    def _(t, oc=oc, b=b, w=w):
                        t.wait_ge(GL[b], c[GLN[b]])
                        t.wait_ge(ev, c["ev"])
                        for cc in range(4):
                            mi = t.matmul(pb2[:, :], wsl[:, b, cc, :],
                                          ctkv[:, cc, 512 * w:512 * w + 512],
                                          start=(cc == 0), stop=(cc == 3))
                        mi.then_inc(mm, 1)
                    c["mm"] += 1

                    @block.scalar
                    def _(sc, oc=oc, w=w):
                        sc.wait_ge(mm, c["mm"])
                        sc.wait_ge(vx, c["vx"])
                        sc.activation(ktc_st[:, :], pb2[:, :], AF.Identity,
                                      bias=buk[:, oc:oc + 1]).then_inc(ev, 1)
                        sc.copy(pstg[:, :],
                                kbf[:, oc, 512 * w:512 * w + 512]).then_inc(ev, 1)
                    c["ev"] += 2

                    rope_slice(pstg[:, :], oc, w,
                               kbf[:, oc, 512 * w:512 * w + 512], ktc_st[:, :])
                wsl_free_mm[b] = c["mm"]

            for j in range(16):
                for n2 in range(2):
                    pband = pb0 if (n2 == 0) else pb1

                    @block.tensor
                    def _(t, j=j, n2=n2, pband=pband, s=s):
                        t.wait_ge(glt, 32)
                        t.wait_ge(ev, c["ev"])
                        if s == 0 and j == 0 and n2 == 0:
                            t.wait_ge(vx, 1)
                        for cc in range(4):
                            t.matmul(pband[:, :], ctkv[:, cc, 128 * j:128 * j + 128],
                                     wuvsb[:, cc, 512 * n2:512 * n2 + 512],
                                     start=(cc == 0), stop=False)
                        t.matmul(pband[:, :], ones_rb[:, :],
                                 buvsb[0:1, 512 * n2:512 * n2 + 512],
                                 start=False, stop=True).then_inc(mm, 1)
                    c["mm"] += 1

                    @block.scalar
                    def _(sc, j=j, n2=n2, pband=pband):
                        sc.wait_ge(mm, c["mm"])
                        for hh in range(8):
                            h = 8 * n2 + hh
                            ci = sc.copy(vaug[:, j, 65 * h:65 * h + 64],
                                         pband[:, 64 * hh:64 * hh + 64])
                        ci.then_inc(ev, 1)
                    c["ev"] += 1

            for w in range(4):
                col0 = s * S + 512 * w

                for oc in range(NC8):
                    b = oc % 2
                    load_wslice(wuq_h, oc, b, nchunk=4)

                    @block.tensor
                    def _(t, oc=oc, b=b, w=w):
                        t.wait_ge(GL[b], c[GLN[b]])
                        t.wait_ge(ev, c["ev"])
                        t.wait_ge(vx, c["vx"])
                        for cc in range(4):
                            mi = t.matmul(pb2[:, :], wsl[:, b, cc, :],
                                          ctq[:, cc, 512 * w:512 * w + 512],
                                          start=(cc == 0), stop=(cc == 3))
                        mi.then_inc(mm, 1)
                    wsl_free_mm[b] = c["mm"] + 1
                    c["mm"] += 1

                    @block.scalar
                    def _(sc, oc=oc):
                        sc.wait_ge(mm, c["mm"])
                        sc.wait_ge(vx, c["vx"])
                        sc.activation(ktc_st[:, :], pb2[:, :], AF.Identity,
                                      bias=buq[:, oc:oc + 1]).then_inc(ev, 1)
                    c["ev"] += 1

                    b2_ = (oc + 1) % 2
                    load_wslice(wqr_h, oc, b2_, nchunk=4)

                    @block.tensor
                    def _(t, oc=oc, b2_=b2_, w=w):
                        t.wait_ge(GL[b2_], c[GLN[b2_]])
                        t.wait_ge(ev, c["ev"])
                        for cc in range(4):
                            mi = t.matmul(pb3[:, :], wsl[:, b2_, cc, :],
                                          ctq[:, cc, 512 * w:512 * w + 512],
                                          start=(cc == 0), stop=(cc == 3))
                        mi.then_inc(mm, 1)
                    wsl_free_mm[b2_] = c["mm"] + 1
                    c["mm"] += 1

                    @block.scalar
                    def _(sc, oc=oc):
                        sc.wait_ge(mm, c["mm"])
                        sc.wait_ge(vx, c["vx"])
                        sc.activation(pstg[:, :], pb3[:, :], AF.Identity,
                                      bias=bqr[:, oc:oc + 1]).then_inc(ev, 1)
                    c["ev"] += 1

                    @block.sync
                    def _(sy, oc=oc, w=w):
                        sy.wait_ge(vx, c["vx"])
                        sy.dma_start(out=tblC[:, :],
                                     in_=rc_h[oc, :, 512 * w:512 * w + 512]).then_inc(lr, 16)
                        sy.dma_start(out=tblS[:, :],
                                     in_=rs_h[oc, :, 512 * w:512 * w + 512]).then_inc(lr, 16)
                    c["lr"] += 32

                    rope_slice(pstg[:, :], oc, w, q_ap(oc), ktc_st[:, :])

                for h in range(16):
                    p0 = 64 * (h % 2)
                    hc = h // 2
                    ev_base = c["ev"]
                    for kt in range(16):
                        pbs = pb0 if (kt % 2 == 0) else pb1

                        @block.tensor
                        def _(t, h=h, kt=kt, w=w, pbs=pbs, p0=p0, hc=hc,
                              ev_base=ev_base):
                            t.wait_ge(vx, c["vx"])
                            if kt < 2:
                                t.wait_ge(ev, ev_base)
                            else:
                                t.wait_ge(ev, ev_base + (kt - 2) + 1)
                            t.matmul(pbs[:, :],
                                     kbf[p0:p0 + 64, hc, 128 * kt:128 * kt + 128],
                                     q_ap(2 * hc + (h % 2), p0, 64),
                                     start=True, stop=True).then_inc(mm, 1)
                        c["mm"] += 1

                        @block.scalar
                        def _(sc, kt=kt, pbs=pbs):
                            sc.wait_ge(mm, c["mm"])
                            sc.activation(E_ap(kt), pbs[:, :], AF.Exp,
                                          scale=SC1).then_inc(ev, 1)
                        c["ev"] += 1

                    @block.tensor
                    def _(t, h=h, w=w):
                        t.wait_ge(ev, c["ev"])
                        t.wait_ge(vx, c["vx"])
                        for kt in range(16):
                            mi = t.matmul(pb3[0:65, :], vaug[:, kt, 65 * h:65 * h + 65],
                                          E_ap(kt), start=(kt == 0), stop=(kt == 15))
                        mi.then_inc(mm, 1)
                    c["mm"] += 1

                    @block.scalar
                    def _(sc, h=h):
                        sc.wait_ge(mm, c["mm"])
                        sc.wait_ge(vx, c["vx"])
                        sc.copy(numst[:, :], pb3[0:64, :]).then_inc(ev, 1)
                        sc.copy(denr[:, :], pb3[64:65, :]).then_inc(ev, 1)
                    c["ev"] += 2

                    @block.vector
                    def _(v, h=h):
                        v.wait_ge(ev, c["ev"])
                        v.reciprocal(denri[:, :], denr[:, :]).then_inc(vx, 1)
                    c["vx"] += 1

                    @block.tensor
                    def _(t, h=h):
                        t.wait_ge(vx, c["vx"])
                        t.matmul(pb2[0:64, :], ones_rf[:, 0:64], denri[:, :],
                                 start=True, stop=True).then_inc(mm, 1)
                    c["mm"] += 1

                    @block.vector
                    def _(v, h=h, p0=p0, hc=hc):
                        v.wait_ge(mm, c["mm"])
                        v.tensor_tensor(obf[p0:p0 + 64, hc, :], numst[:, :],
                                        pb2[0:64, :], ALU.mult).then_inc(vx, 1)
                    c["vx"] += 1

                @block.sync
                def _(sy, w=w, col0=col0):
                    sy.wait_ge(vx, c["vx"])
                    sy.wait_ge(stS, c["stS"])
                    sy.wait_ge(stG, c["stG"])
                    sy.dma_start(out=zpre[:, :, :],
                                 in_=z2_h[:, :, col0:col0 + 512]
                                 .rearrange("c p t -> p c t")).then_inc(lr, 16)
                c["lr"] += 16

                for oc in range(NC8):
                    b = oc % 2
                    load_wslice(wo1_h, oc, b)

                    @block.tensor
                    def _(t, oc=oc, b=b):
                        t.wait_ge(GL[b], c[GLN[b]])
                        t.wait_ge(vx, c["vx"])
                        t.wait_ge(ev, c["ev"])
                        for cc in range(NC8):
                            mi = t.matmul(pb2[:, :], wsl[:, b, cc, :], obf[:, cc, :],
                                          start=(cc == 0), stop=(cc == NC8 - 1))
                        mi.then_inc(mm, 1)
                    wsl_free_mm[b] = c["mm"] + 1
                    c["mm"] += 1

                    @block.scalar
                    def _(sc, oc=oc):
                        sc.wait_ge(mm, c["mm"])
                        sc.wait_ge(vx, c["vx"])
                        sc.activation(wo_st[:, :], pb2[:, :], AF.Identity,
                                      bias=wob[:, oc:oc + 1]).then_inc(ev, 1)
                    c["ev"] += 1

                    @block.vector
                    def _(v, oc=oc):
                        v.wait_ge(ev, c["ev"])
                        v.wait_ge(lr, c["lr"])
                        v.tensor_tensor(zpre[:, oc, :], zpre[:, oc, :],
                                        wo_st[:, :], ALU.add).then_inc(vx, 1)
                    c["vx"] += 1

                emit_ln_stripe(block, B, c, col0, z3_h, z3b_h)

        @block.sync
        def _(sy):
            sy.wait_ge(stS, c["stS"])
            sy.wait_ge(stG, c["stG"])
    return nc



def build_launch_d():
    nc = bass.Bass("TRN2", target_bir_lowering=False)
    z3_h = nc.dram_tensor("z3_fm", [NC8, 128, T], F32, kind="ExternalInput").ap()
    z3b_h = nc.dram_tensor("z3_bf", [NC8, 128, T], BF16, kind="ExternalInput").ap()
    rm_h = nc.dram_tensor("rmT_bd", [4, NC8, 128, 128], F32, kind="ExternalInput").ap()
    rup_h = nc.dram_tensor("rupT_bd", [4, NC8, 128, 512], F32, kind="ExternalInput").ap()
    wc_h = nc.dram_tensor("wc_t", [128, 128, D], F32, kind="ExternalInput").ap()
    onc_h = nc.dram_tensor("ones_col_bf", [128, 1], BF16, kind="ExternalInput").ap()
    onrf_h = nc.dram_tensor("ones_row_f", [1, 128], F32, kind="ExternalInput").ap()
    h_hbm = nc.dram_tensor("h_d", [128, 128, T], BF16, kind="ExternalOutput").ap()
    acc_h = nc.dram_tensor("acc_d", [NC8, 128, T], F32, kind="ExternalOutput").ap()
    out_h = nc.dram_tensor("out_fm", [NC8, 128, T], F32, kind="ExternalOutput").ap()

    import contextlib
    es = contextlib.ExitStack()
    def sb(name, shape, dtype):
        return es.enter_context(nc.sbuf_tensor(name, shape, dtype))
    def psf(name):
        return es.enter_context(nc.psum_tensor(name, [128, 512], F32))
    def sem(name):
        return es.enter_context(nc.semaphore(name))
    G_sb = sb("G_sb", [128, 4, NC8, 512], BF16)
    hst = sb("hst", [128, 32, 512], BF16)
    z3w = sb("z3w", [128, NC8, 512], BF16)
    wce = sb("wce", [128, 16, D], BF16)
    he = sb("he", [128, 16, 512], BF16)
    ysb = sb("ysb", [128, NC8, 512], F32)
    rmsl = sb("rmsl", [128, 128], BF16)
    rupsl = sb("rupsl", [128, 512], BF16)
    ones_col = sb("ones_col_sb", [128, 1], BF16)
    ones_rf = sb("ones_row_sb", [1, 128], F32)
    epsc = sb("epsc", [1, 1], F32)
    zpre = sb("zpre", [128, NC8, 512], F32)
    zbf = sb("zbf", [128, NC8, 512], BF16)
    zsq = sb("zsq", [128, NC8, 512], BF16)
    row_mean = sb("row_mean", [1, 512], F32)
    row_ms = sb("row_ms", [1, 512], F32)
    row_var = sb("row_var", [1, 512], F32)
    row_r = sb("row_r", [1, 512], F32)
    row_tmp = sb("row_tmp", [1, 512], F32)
    pb0, pb1, pb2, pb3 = psf("pb0"), psf("pb1"), psf("pb2"), psf("pb3")
    pb4, pb5, pb6, pb7 = psf("pb4"), psf("pb5"), psf("pb6"), psf("pb7")
    ld, lr, mm, ev, vx = sem("ld"), sem("lr"), sem("mm"), sem("ev"), sem("vx")
    glA, glB = sem("glA"), sem("glB")
    stS, stG = sem("stS"), sem("stG")

    with es, nc.Block() as block:
        c = dict(ld=0, lr=0, mm=0, ev=0, vx=0, glA=0, glB=0, stS=0, stG=0)

        @block.sync
        def _(sy):
            sy.dma_start(out=ones_col[:, :], in_=onc_h).then_inc(ld, 16)
            sy.dma_start(out=ones_rf[:, :], in_=onrf_h).then_inc(ld, 16)
        c["ld"] += 32
        LD_STATIC = c["ld"]

        @block.vector
        def _(v):
            v.memset(epsc[:, :], EPS).then_inc(vx, 1)
        c["vx"] += 1

        B = dict(zpre=zpre, zbf=zbf, zsq=zsq, row_mean=row_mean, row_ms=row_ms,
                 row_var=row_var, row_r=row_r, row_tmp=row_tmp,
                 ones_col_bf=ones_col, ones_row_f=ones_rf, epsc=epsc,
                 pS=pb6, pQ=pb7, pR=pb4, pM=pb5, ld_static=LD_STATIC,
                 s_ld=ld, s_mm=mm, s_ev=ev, s_vx=vx, s_stS=stS, s_stG=stG)

        for h4 in range(4):
            for cc in range(NC8):
                @block.gpsimd
                def _(g, h4=h4, cc=cc):
                    g.wait_ge(mm, c["mm"])
                    g.dma_start(out=rmsl[:, :], in_=rm_h[h4, cc, :, :]).then_inc(glA, 16)
                    g.dma_start(out=rupsl[:, :], in_=rup_h[h4, cc, :, :]).then_inc(glA, 16)
                c["glA"] += 32

                @block.tensor
                def _(t, h4=h4, cc=cc):
                    t.wait_ge(glA, c["glA"])
                    t.wait_ge(ev, c["ev"])
                    t.matmul(pb0[:, :], rmsl[:, :], rupsl[:, :],
                             start=True, stop=True).then_inc(mm, 1)
                c["mm"] += 1

                @block.scalar
                def _(sc, h4=h4, cc=cc):
                    sc.wait_ge(mm, c["mm"])
                    sc.copy(G_sb[:, h4, cc, :], pb0[:, :]).then_inc(ev, 1)
                c["ev"] += 1

        NW = T // 512
        for w in range(NW):
            @block.sync
            def _(sy, w=w):
                sy.wait_ge(mm, c["mm"])
                sy.dma_start(out=z3w[:, :, :],
                             in_=z3b_h[:, :, 512 * w:512 * w + 512]
                             .rearrange("c p t -> p c t")).then_inc(lr, 16)
            c["lr"] += 16

            for h4 in range(4):
                ev_base = c["ev"]
                for cc in range(NC8):
                    for sub in range(4):
                        idx = 4 * cc + sub
                        pbh = pb0 if (idx % 2 == 0) else pb1

                        @block.tensor
                        def _(t, w=w, h4=h4, cc=cc, sub=sub, idx=idx, pbh=pbh,
                              ev_base=ev_base):
                            t.wait_ge(lr, c["lr"])
                            if idx < 2:
                                t.wait_ge(stG, c["stG"])
                                t.wait_ge(ev, ev_base)
                            else:
                                t.wait_ge(ev, ev_base + (idx - 2) + 1)
                            t.matmul(pbh[:, :], G_sb[:, h4, cc, 128 * sub:128 * sub + 128],
                                     z3w[:, cc, :], start=True,
                                     stop=True).then_inc(mm, 1)
                        c["mm"] += 1

                        @block.scalar
                        def _(sc, idx=idx, pbh=pbh):
                            sc.wait_ge(mm, c["mm"])
                            sc.activation(hst[:, idx, :], pbh[:, :],
                                          AF.Gelu).then_inc(ev, 1)
                        c["ev"] += 1

                @block.gpsimd
                def _(g, w=w, h4=h4):
                    g.wait_ge(ev, c["ev"])
                    g.dma_start(out=h_hbm[32 * h4:32 * h4 + 32, :, 512 * w:512 * w + 512]
                                .rearrange("m p t -> p m t"),
                                in_=hst[:, :, :]).then_inc(stG, 16)
                c["stG"] += 16

        wce_free_mm = 0
        he_free_mm = 0
        for ei in range(8):
            @block.gpsimd
            def _(g, ei=ei):
                g.wait_ge(mm, wce_free_mm)
                g.dma_start(out=wce[:, :, :],
                            in_=wc_h[16 * ei:16 * ei + 16, :, :]
                            .rearrange("m p d -> p m d")).then_inc(glB, 16)
            c["glB"] += 16

            for w in range(NW):
                @block.sync
                def _(sy, ei=ei, w=w):
                    sy.wait_ge(mm, he_free_mm)
                    sy.wait_ge(stG, c["stG"])
                    sy.dma_start(out=he[:, :, :],
                                 in_=h_hbm[16 * ei:16 * ei + 16, :, 512 * w:512 * w + 512]
                                 .rearrange("m p t -> p m t")).then_inc(lr, 16)
                c["lr"] += 16

                if ei == 0:
                    @block.sync
                    def _(sy, w=w):
                        sy.wait_ge(stG, c["stG"])
                        sy.wait_ge(vx, c["vx"])
                        sy.dma_start(out=ysb[:, :, :],
                                     in_=z3_h[:, :, 512 * w:512 * w + 512]
                                     .rearrange("c p t -> p c t")).then_inc(lr, 16)
                    c["lr"] += 16

                for oc in range(NC8):
                    pby = pb2 if (oc % 2 == 0) else pb3

                    @block.tensor
                    def _(t, ei=ei, w=w, oc=oc, pby=pby):
                        t.wait_ge(glB, c["glB"])
                        t.wait_ge(lr, c["lr"])
                        t.wait_ge(ev, c["ev"])
                        t.wait_ge(vx, c["vx"])
                        for mc in range(16):
                            mi = t.matmul(pby[:, :], wce[:, mc, 128 * oc:128 * oc + 128],
                                          he[:, mc, :], start=(mc == 0),
                                          stop=(mc == 15))
                        mi.then_inc(mm, 1)
                    c["mm"] += 1

                    if ei == 0:
                        @block.vector
                        def _(v, w=w, oc=oc, pby=pby):
                            v.wait_ge(mm, c["mm"])
                            v.wait_ge(lr, c["lr"])
                            v.wait_ge(stG, c["stG"])
                            v.tensor_tensor(ysb[:, oc, :], ysb[:, oc, :],
                                            pby[:, :], ALU.add).then_inc(vx, 1)
                        c["vx"] += 1
                    else:
                        @block.scalar
                        def _(sc, w=w, oc=oc, pby=pby):
                            sc.wait_ge(mm, c["mm"])
                            sc.wait_ge(stG, c["stG"])
                            sc.copy(ysb[:, oc, :], pby[:, :]).then_inc(ev, 1)
                        c["ev"] += 1

                @block.gpsimd
                def _(g, ei=ei, w=w):
                    g.wait_ge(vx, c["vx"])
                    g.wait_ge(ev, c["ev"])
                    g.wait_ge(stG, c["stG"])
                    if ei == 0:
                        g.dma_start(out=acc_h[:, :, 512 * w:512 * w + 512]
                                    .rearrange("c p t -> p c t"),
                                    in_=ysb[:, :, :]).then_inc(stG, 16)
                    else:
                        g.dma_start(out=acc_h[:, :, 512 * w:512 * w + 512]
                                    .rearrange("c p t -> p c t"),
                                    in_=ysb[:, :, :],
                                    accum_op=ALU.add).then_inc(stG, 16)
                c["stG"] += 16

                he_free_mm = c["mm"]
            wce_free_mm = c["mm"]

        for w in range(NW):
            @block.sync
            def _(sy, w=w):
                sy.wait_ge(stG, c["stG"])
                if w >= 1:
                    sy.wait_ge(stS, c["stS"])
                    sy.wait_ge(mm, c["mm"])
                sy.dma_start(out=zpre[:, :, :],
                             in_=acc_h[:, :, 512 * w:512 * w + 512]
                             .rearrange("c p t -> p c t")).then_inc(lr, 16)
            c["lr"] += 16
            B["zpre_ready"] = (lr, c["lr"])
            emit_ln_stripe(block, B, c, 512 * w, out_h, None)

        @block.sync
        def _(sy):
            sy.wait_ge(stS, c["stS"])
            sy.wait_ge(stG, c["stG"])
    return nc



_A_KEYS = ["l0_wq_t", "l0_wk_t", "l0_wv_t", "l0_wg_t", "l0_wo_t", "l0_lepe_w",
           "l0_lepe_b", "l0_ropec", "l0_ropes", "iden_bf", "ones_col_bf",
           "ones_row_f"]
_B_KEYS = ["gate_w", "gate_b", "w1_t", "b1_r", "w2_t", "b2_r", "sel_bf",
           "iden_bf", "ones_col_bf", "ones_row_f"]
_C_KEYS = ["wdq_t", "bdq_c", "wuq_t", "buq_c", "wqr_t", "bqr_c", "wkr_t",
           "bkr_c", "wdkv_t", "bdkv_c", "wuk_t", "buk_c", "wuv_t", "buv_r",
           "wo1_t", "wob_c", "l1_ropec", "l1_ropes", "iden_bf", "ones_col_bf",
           "ones_row_bf", "ones_row_f"]
_D_KEYS = ["rmT_bd", "rupT_bd", "wc_t", "ones_col_bf", "ones_row_f"]


def kernel(**inputs):
    ncores = 8
    sh = prep_shared(inputs)
    x = np.asarray(inputs["L0_attn_hidden_states"])
    cores = list(range(ncores))

    ncA = build_launch_a()
    maps = [dict({k: sh[k] for k in _A_KEYS}, x_fm=prep_core_x(x, cc))
            for cc in cores]
    resA = run_bass_kernel_spmd(ncA, maps, core_ids=cores)
    z1f = [resA.results[cc]["z1_fm"].reshape(NC8, 128, T) for cc in cores]
    z1b = [resA.results[cc]["z1_bf"].reshape(NC8, 128, T) for cc in cores]

    ncB = build_launch_b()
    maps = [dict({k: sh[k] for k in _B_KEYS}, z1_fm=z1f[cc], z1_bf=z1b[cc])
            for cc in cores]
    resB = run_bass_kernel_spmd(ncB, maps, core_ids=cores)
    z2f = [resB.results[cc]["z2_fm"].reshape(NC8, 128, T) for cc in cores]
    z2b = [resB.results[cc]["z2_bf"].reshape(NC8, 128, T) for cc in cores]

    ncC = build_launch_c()
    maps = [dict({k: sh[k] for k in _C_KEYS}, z2_fm=z2f[cc], z2_bf=z2b[cc])
            for cc in cores]
    resC = run_bass_kernel_spmd(ncC, maps, core_ids=cores)
    z3f = [resC.results[cc]["z3_fm"].reshape(NC8, 128, T) for cc in cores]
    z3b = [resC.results[cc]["z3_bf"].reshape(NC8, 128, T) for cc in cores]

    ncD = build_launch_d()
    maps = [dict({k: sh[k] for k in _D_KEYS}, z3_fm=z3f[cc], z3_bf=z3b[cc])
            for cc in cores]
    resD = run_bass_kernel_spmd(ncD, maps, core_ids=cores)
    outs = [resD.results[cc]["out_fm"] for cc in cores]
    return finish_out(outs)
